# version 13
# baseline (speedup 1.0000x reference)
import contextlib
import numpy as np

import concourse.bass as bass
import concourse.mybir as mybir
from concourse.bass_utils import run_bass_kernel_spmd

F32 = mybir.dt.float32
BF16 = mybir.dt.bfloat16
AF = mybir.ActivationFunctionType
ALU = mybir.AluOpType

D = 1024
HID = 4096
DC = D // 128
HC = HID // 128
TS = 512
TF = 256
EPS = 1e-5
INVD = 1.0 / D

ENGS = ("sync", "gpsimd", "vector", "scalar", "tensor")


class Plan:
    def __init__(self):
        self.ops = {e: [] for e in ENGS}

    def em(self, eng, fn):
        self.ops[eng].append(fn)

    def emit(self, nc):
        with nc.Block() as block:
            @block.sync
            def _(eng):
                for f in self.ops["sync"]:
                    f(eng)

            @block.gpsimd
            def _(eng):
                for f in self.ops["gpsimd"]:
                    f(eng)

            @block.vector
            def _(eng):
                for f in self.ops["vector"]:
                    f(eng)

            @block.scalar
            def _(eng):
                for f in self.ops["scalar"]:
                    f(eng)

            @block.tensor
            def _(eng):
                for f in self.ops["tensor"]:
                    f(eng)


class Sems:
    def __init__(self, nc, stack, names):
        self.h = {n: stack.enter_context(nc.semaphore(n)) for n in names}
        self.c = {n: 0 for n in names}

    def inc(self, name, n=1):
        self.c[name] += n
        return self.h[name]

    def cur(self, name):
        return self.c[name]


def build_program(pre_tiles=8, own_tiles=8):
    ffn_tiles = own_tiles * (TS // TF)
    n_tiles = pre_tiles + own_tiles
    tok_in = n_tiles * TS
    tok_own = own_tiles * TS

    nc = bass.Bass("TRN2", target_bir_lowering=False, debug=False)

    x_h = nc.dram_tensor("x", [tok_in, D], F32, kind="ExternalInput").ap()
    bT_h = nc.dram_tensor("bT", [D, D], F32, kind="ExternalInput").ap()
    wT_h = nc.dram_tensor("wT", [D, D], F32, kind="ExternalInput").ap()
    ach_h = nc.dram_tensor("Ach", [128, DC], F32, kind="ExternalInput").ap()
    bch_h = nc.dram_tensor("bch", [128, DC], F32, kind="ExternalInput").ap()
    wsrc, wcache = {}, {}
    for nm, sz in (("w1s", DC * HID * 128), ("w2s", DC * HID * 128),
                   ("h1s", HC * HID * 128), ("h2s", HC * HID * 128),
                   ("w3s", HC * D * 128)):
        wsrc[nm] = nc.dram_tensor(nm, [sz], F32, kind="ExternalInput").ap()
        wcache[nm] = nc.dram_tensor(nm + "_bf", [sz], BF16, kind="Internal").ap()
    sc_h = {}
    for nm, cols in (("w1sc", HC), ("w2sc", HC), ("h1sc", HC), ("h2sc", HC),
                     ("w3sc", DC)):
        sc_h[nm] = nc.dram_tensor(nm, [128, cols], F32, kind="ExternalInput").ap()
    x2sp_h = nc.dram_tensor("x2sp", [tok_own, D], F32, kind="Internal").ap()
    out_h = nc.dram_tensor("out", [tok_own, D], F32, kind="ExternalOutput").ap()

    with contextlib.ExitStack() as st:
        S = Sems(nc, st, [
            "ident", "init_dma", "bbf", "lnhi", "x_dma0", "x_dma1", "x_dma2",
            "xfree",
            "cast_w1s", "cast_w2s", "cast_h1s", "cast_h2s", "cast_w3s",
            "nrm_v", "nrm_s",
            "lnT", "lnTev", "m1u", "scan", "m2u", "attnev", "attnT",
            "attnTev", "x2done", "spill0", "spill1", "phase",
            "x2l0", "x2l1", "wdma0", "wdma1", "wdma2", "wuse", "ln2sp",
            "s1", "s2", "siluev", "gev", "gsp_v", "gsp_s",
            "m5u", "m5ev", "m6u", "m6ev", "m7u", "m7ev",
            "fT", "ftev", "outrdy", "ost",
        ])

        sb = lambda name, shape, dt=F32: st.enter_context(
            nc.sbuf_tensor(name, shape, dt))
        ident = sb("ident", [128, 128])
        A_sb = sb("A_sb", [128, DC])
        b_sb = sb("b_sb", [128, DC])
        scsb = {nm: sb(nm + "_sb", [128, sc_h[nm].shape[1]]) for nm in sc_h}
        ms4 = sb("ms4", [128, 8])
        ss4 = sb("ss4", [128, 8])
        istd4 = sb("istd4", [128, 8])
        sqscr = sb("sqscr", [128, D])
        carry0 = sb("carry0", [128, 1])
        pb = [st.enter_context(nc.psum_tensor(f"pb{i}", [128, 512], F32))
              for i in range(8)]

        P = Plan()

        def W(eng, name, tgt, plan=None):
            if tgt >= 1:
                h = S.h[name]
                (plan or P).em(eng, lambda e, h=h, t=tgt: e.wait_ge(h, t))

        def NOPINC(eng, name):
            h = S.inc(name)
            t = S.cur(name)
            P.em(eng, lambda e, h=h: e.drain().then_inc(h, 1))
            return t

        def norm_tm(srcs, dsts):
            n = len(srcs)
            P.em("vector", lambda e: e.drain())
            for c in range(n):
                P.em("vector", lambda e, s_=srcs[c], c=c: e.reduce_sum(
                    ms4[:, c:c + 1], s_, axis=mybir.AxisListType.X))
            for c in range(n):
                scr = dsts[c] if dsts[c] is not srcs[c] else sqscr[:, :]
                if dsts[c] is srcs[c] and c > 0:
                    P.em("vector", lambda e: e.drain())
                P.em("vector", lambda e, s_=srcs[c], d=scr, c=c:
                     e.scalar_tensor_tensor(
                         d, s_, 1.0, s_, ALU.mult, ALU.mult,
                         accum_out=ss4[:, c:c + 1]))
            P.em("vector", lambda e: e.drain())
            P.em("vector", lambda e, n=n: e.tensor_scalar(
                ms4[:, 0:n], ms4[:, 0:n], INVD, None, ALU.mult))
            P.em("vector", lambda e, n=n: e.tensor_scalar(
                ss4[:, 0:n], ss4[:, 0:n], INVD, EPS, ALU.mult, ALU.add))
            P.em("vector", lambda e: e.drain())
            P.em("vector", lambda e, n=n: e.tensor_mul(
                istd4[:, 0:n], ms4[:, 0:n], ms4[:, 0:n]))
            P.em("vector", lambda e: e.drain())
            hv = S.inc("nrm_v")
            P.em("vector", lambda e, h=hv, n=n: e.tensor_sub(
                ss4[:, 0:n], ss4[:, 0:n], istd4[:, 0:n]).then_inc(h, 1))
            W("scalar", "nrm_v", S.cur("nrm_v"))
            hs = S.inc("nrm_s")
            P.em("scalar", lambda e, h=hs, n=n: e.activation(
                ss4[:, 0:n], ss4[:, 0:n], AF.Sqrt).then_inc(h, 1))
            W("vector", "nrm_s", S.cur("nrm_s"))
            P.em("vector", lambda e, n=n: e.reciprocal(
                istd4[:, 0:n], ss4[:, 0:n]))
            P.em("vector", lambda e: e.drain())
            for c in range(n):
                P.em("vector", lambda e, s_=srcs[c], d=dsts[c], c=c:
                     e.tensor_scalar(d, s_, ms4[:, c:c + 1],
                                     istd4[:, c:c + 1], ALU.subtract,
                                     ALU.mult))

        inner = contextlib.ExitStack()
        sbi = lambda name, shape, dt=F32: inner.enter_context(
            nc.sbuf_tensor(name, shape, dt))
        colidx = sbi("colidx", [128, 128])
        rowidx = sbi("rowidx", [128, 1])
        xb = sbi("xb", [128, 3 * 4096])
        tmbuf = sbi("tmbuf", [128, 4096])
        lfm = sbi("lfm", [128, 4096])
        afm = lfm
        h2b = sbi("h2b", [128, 4096])
        hcar = sbi("hcar", [128, DC])
        tmb_ln = sbi("tmb_ln", [128, 4096])
        BT_sb = sbi("BT_sb", [128, DC * D])
        WT_sb = sbi("WT_sb", [128, DC * D])
        B_bf = sbi("B_bf", [128, DC * D], BF16)
        lfm_hi = sbi("lfm_hi", [128, 4096], BF16)
        far_tiles = set(range(max(0, pre_tiles - 2)))

        P.em("gpsimd", lambda e: e.iota(colidx[:, :], [[1, 128]],
                                        channel_multiplier=0,
                                        allow_small_or_imprecise_dtypes=True))
        hI = S.inc("ident")
        P.em("gpsimd", lambda e, h=hI: e.iota(
            rowidx[:, :], [[1, 1]], channel_multiplier=1,
            allow_small_or_imprecise_dtypes=True).then_inc(h, 1))
        ncast_gate = min(n_tiles, 10)
        for sl in range(3):
            cnt = len([i for i in range(ncast_gate) if i % 3 == sl])
            if cnt:
                P.em("gpsimd", lambda e, h=S.h[f"x_dma{sl}"], t=16 * cnt:
                     e.wait_ge(h, t))
        for nm in ("w1s", "w2s", "h1s", "h2s", "w3s"):
            src = wsrc[nm].rearrange("(a b) -> a b", a=16)
            dst = wcache[nm].rearrange("(a b) -> a b", a=16)
            hC = S.inc("cast_" + nm, 16)
            P.em("gpsimd", lambda e, s_=src, d=dst, h=hC: e.dma_start(
                d, s_).then_inc(h, 16))

        W("vector", "ident", 1)
        hI2 = S.inc("ident")
        P.em("vector", lambda e, h=hI2: e.tensor_scalar(
            ident[:, :], colidx[:, :], rowidx[:, 0:1], None,
            ALU.is_equal).then_inc(h, 1))
        P.em("vector", lambda e: e.memset(carry0[:, :], 0.0))
        P.em("vector", lambda e: e.drain())

        def init_dma(dst_ap, src_ap):
            h = S.inc("init_dma", 16)
            P.em("sync", lambda e, d=dst_ap, s_=src_ap, h=h: e.dma_start(
                d, s_).then_inc(h, 16))

        for k in range(DC):
            init_dma(BT_sb[:, k * D:(k + 1) * D], bT_h[k * 128:(k + 1) * 128, :])
        for k in range(DC):
            init_dma(WT_sb[:, k * D:(k + 1) * D], wT_h[k * 128:(k + 1) * 128, :])
        init_dma(A_sb[:, :], ach_h[:, :])
        init_dma(b_sb[:, :], bch_h[:, :])
        for nm in sc_h:
            init_dma(scsb[nm][:, :], sc_h[nm][:, :])
        init_tgt = S.cur("init_dma")
        for eng in ("vector", "scalar", "tensor"):
            W(eng, "init_dma", init_tgt)
        hBf = S.inc("bbf")
        P.em("vector", lambda e, h=hBf: e.tensor_copy(
            B_bf[:, :], BT_sb[:, :]).then_inc(h, 1))

        def xload(i):
            if i >= n_tiles:
                return
            dst = xb[:, (i % 3) * 4096:(i % 3 + 1) * 4096].rearrange(
                "p (t d) -> p t d", t=4)
            src = x_h[i * TS:(i + 1) * TS, :].rearrange("(t p) d -> p t d", p=128)
            W("sync", "xfree", i - 2)
            if i - 3 >= pre_tiles:
                t = i - 3 - pre_tiles
                W("sync", f"spill{t % 2}", 16 * (t // 2 + 1))
            h = S.inc(f"x_dma{i % 3}", 16)
            P.em("sync", lambda e, d=dst, s_=src, h=h: e.dma_start(
                d, s_).then_inc(h, 16))

        xload(0)
        xload(1)

        ln1_nv = {}
        lnb = [tmb_ln[:, c * D:(c + 1) * D] for c in range(4)]

        def ln1_for(i, extra_srcs=None, extra_dsts=None):
            sx2 = i % 3
            W("vector", f"x_dma{sx2}", 16 * (i // 3 + 1))
            W("vector", "lnT", 8 * i)
            P.em("vector", lambda e: e.drain())
            xsrc = [xb[:, sx2 * 4096 + c * D: sx2 * 4096 + (c + 1) * D]
                    for c in range(4)]
            norm_tm((extra_srcs or []) + xsrc, (extra_dsts or []) + lnb)
            ln1_nv[i] = NOPINC("vector", "nrm_v")
            if i < pre_tiles:
                NOPINC("vector", "xfree")

        ln1_for(0)

        for i in range(n_tiles):
            s = i % 2
            is_own = i >= pre_tiles
            oi = i - pre_tiles
            if not is_own:
                xload(i + 2)
            sx = i % 3

            W("tensor", "nrm_v", ln1_nv[i])
            if i == 0:
                W("tensor", "ident", 2)
            for fc in range(DC):
                bank = pb[4 + fc % 2]
                W("tensor", "lnTev", 8 * i + fc - 1)
                for tc in range(4):
                    src = tmb_ln[:, tc * D + fc * 128: tc * D + (fc + 1) * 128]
                    dst = bank[:, tc * 128:(tc + 1) * 128]
                    if tc < 3:
                        P.em("tensor", lambda e, d=dst, s_=src: e.transpose(
                            d, s_, ident[:, :]))
                    else:
                        hT = S.inc("lnT")
                        P.em("tensor", lambda e, d=dst, s_=src, h=hT:
                             e.transpose(d, s_, ident[:, :]).then_inc(h, 1))
                W("vector", "lnT", S.cur("lnT"))
                if i > 0 and fc == 0:
                    W("vector", "m1u", 8 * i)
                    if i - 1 >= pre_tiles:
                        W("vector", "attnT", 4 * (i - 1 - pre_tiles + 1))
                far = i in far_tiles
                dstf = (lfm_hi if far else lfm)[:, fc * 512:(fc + 1) * 512]
                hE = S.inc("lnTev")
                P.em("vector", lambda e, d=dstf, b=bank, h=hE: e.tensor_copy(
                    d, b[:, 0:512]).then_inc(h, 1))

            W("tensor", "lnTev", S.cur("lnTev"))
            if i in far_tiles and (i == 0 or i - 1 not in far_tiles):
                W("tensor", "bbf", 1)
            for m in range(DC):
                bank = pb[m % 2]
                W("tensor", "scan", 8 * i + m - 1)
                for k in range(DC):
                    if i in far_tiles:
                        lhs = B_bf[:, k * D + m * 128: k * D + (m + 1) * 128]
                        rhs = lfm_hi[:, k * 512:(k + 1) * 512]
                    else:
                        lhs = BT_sb[:, k * D + m * 128: k * D + (m + 1) * 128]
                        rhs = lfm[:, k * 512:(k + 1) * 512]
                    hM = S.inc("m1u") if k == DC - 1 else None
                    def mm1(e, o=bank, l=lhs, r=rhs, f=(k == 0),
                            la=(k == DC - 1), h=hM):
                        ins = e.matmul(o[:, 0:512], l, r, start=f, stop=la)
                        if h is not None:
                            ins.then_inc(h, 1)
                    P.em("tensor", mm1)
                W("vector", "m1u", S.cur("m1u"))
                if m == 0:
                    P.em("vector", lambda e: e.drain())
                    if i >= 1 and (i - 1) >= pre_tiles:
                        W("vector", "m2u", 8 * (i - 1 - pre_tiles + 1))
                dsth = h2b[:, m * 512:(m + 1) * 512]
                car = carry0[:, 0:1] if i == 0 else hcar[:, m:m + 1]
                abc = A_sb[:, m:m + 1].broadcast_to([128, 512])
                hS = S.inc("scan")
                P.em("vector", lambda e, d=dsth, a=abc, u=bank, c=car, h=hS:
                     e.tensor_tensor_scan(d, a, u[:, 0:512], c, ALU.mult,
                                          ALU.add).then_inc(h, 1))
                W("vector", "scan", S.cur("scan"))
                P.em("vector", lambda e, d=hcar[:, m:m + 1],
                     s_=h2b[:, m * 512 + 511: m * 512 + 512]:
                     e.tensor_copy(d, s_))

            if not is_own:
                if i + 1 < n_tiles:
                    ln1_for(i + 1)
                continue

            W("tensor", "scan", S.cur("scan"))
            for m in range(DC):
                bank = pb[2 + m % 2]
                W("tensor", "attnev", 8 * oi + m - 1)
                for k in range(DC):
                    lhs = WT_sb[:, k * D + m * 128: k * D + (m + 1) * 128]
                    rhs = h2b[:, k * 512:(k + 1) * 512]
                    hM = S.inc("m2u") if k == DC - 1 else None
                    def mm2(e, o=bank, l=lhs, r=rhs, f=(k == 0),
                            la=(k == DC - 1), h=hM):
                        ins = e.matmul(o[:, 0:512], l, r, start=f, stop=la)
                        if h is not None:
                            ins.then_inc(h, 1)
                    P.em("tensor", mm2)
                W("scalar", "m2u", S.cur("m2u"))
                if oi > 0 and m == 0:
                    W("scalar", "attnT", 4 * oi)
                dsta = afm[:, m * 512:(m + 1) * 512]
                hA = S.inc("attnev")
                P.em("scalar", lambda e, d=dsta, b=bank, m=m, h=hA:
                     e.activation(d, b[:, 0:512], AF.Identity,
                                  bias=b_sb[:, m:m + 1]).then_inc(h, 1))

            W("tensor", "attnev", S.cur("attnev"))
            for tc in range(4):
                W("tensor", "attnTev", 4 * oi + tc)
                for fc in range(DC):
                    bank = pb[6] if fc < 4 else pb[7]
                    src = afm[:, fc * 512 + tc * 128: fc * 512 + (tc + 1) * 128]
                    dst = bank[:, (fc % 4) * 128:(fc % 4 + 1) * 128]
                    if fc < DC - 1:
                        P.em("tensor", lambda e, d=dst, s_=src: e.transpose(
                            d, s_, ident[:, :]))
                    else:
                        hT = S.inc("attnT")
                        P.em("tensor", lambda e, d=dst, s_=src, h=hT:
                             e.transpose(d, s_, ident[:, :]).then_inc(h, 1))
                W("vector", "attnT", S.cur("attnT"))
                if tc == 0:
                    P.em("vector", lambda e: e.drain())
                d0 = tmbuf[:, tc * D: tc * D + 512]
                d1 = tmbuf[:, tc * D + 512: tc * D + 1024]
                P.em("vector", lambda e, d=d0: e.tensor_copy(d, pb[6][:, 0:512]))
                hE = S.inc("attnTev")
                P.em("vector", lambda e, d=d1, h=hE: e.tensor_copy(
                    d, pb[7][:, 0:512]).then_inc(h, 1))

            atb = [tmbuf[:, c * D:(c + 1) * D] for c in range(4)]
            if i + 1 < n_tiles:
                ln1_for(i + 1, extra_srcs=atb, extra_dsts=atb)
            else:
                norm_tm(atb, atb)
            P.em("vector", lambda e: e.drain())
            for c in range(4):
                xc = xb[:, sx * 4096 + c * D: sx * 4096 + (c + 1) * D]
                P.em("vector", lambda e, a=xc, b_=atb[c]: e.tensor_tensor(
                    a, a, b_, ALU.add))
            x2t = NOPINC("vector", "x2done")
            NOPINC("vector", "xfree")

            W("sync", "x2done", x2t)
            W("sync", f"spill{oi % 2}", 16 * (oi // 2))
            dst = x2sp_h[oi * TS:(oi + 1) * TS, :].rearrange(
                "(t p) d -> p t d", p=128)
            src = xb[:, sx * 4096:(sx + 1) * 4096].rearrange(
                "p (t d) -> p t d", t=4)
            hSp = S.inc(f"spill{oi % 2}", 16)
            P.em("sync", lambda e, d=dst, s_=src, h=hSp: e.dma_start(
                d, s_).then_inc(h, 16))
            xload(i + 2)

        for eng in ("vector", "scalar", "tensor"):
            NOPINC(eng, "phase")
        phase_tgt = S.cur("phase")
        spill_tot = {0: 16 * ((own_tiles + 1) // 2), 1: 16 * (own_tiles // 2)}

        P.emit(nc)
        inner.close()

        P = Plan()
        WSLOTS = 3
        wblk = sb("wblk", [128, WSLOTS * 8192], BF16)
        x2f = sb("x2f", [128, 2 * 2048])
        ln2tm = sb("ln2tm", [128, 2048])
        f_tm = sb("f_tm", [128, 2048])
        l2hi = sb("l2hi", [128, DC * TF], BF16)
        l2mid = sb("l2mid", [128, DC * TF], BF16)
        l2lo = sb("l2lo", [128, DC * TF], BF16)
        gahi = sb("gahi", [128, HC * TF], BF16)
        gamid = sb("gamid", [128, HC * TF], BF16)
        galo = sb("galo", [128, HC * TF], BF16)
        gbhi = sb("gbhi", [128, HC * TF], BF16)
        gbmid = sb("gbmid", [128, HC * TF], BF16)
        gblo = sb("gblo", [128, HC * TF], BF16)
        gscr = sb("gscr", [128, 2 * TF])
        r1s = sb("r1s", [128, 2 * TF])
        r2s = sb("r2s", [128, 2 * TF])
        f_fm = sb("f_fm", [128, DC * TF])
        silub = f_fm

        blk_idx = [0]
        blk_done = {}

        def wload(nm, goff, ncols):
            k = blk_idx[0]
            slot = k % WSLOTS
            if k == 0:
                W("sync", "phase", phase_tgt)
                W("sync", "spill0", spill_tot[0])
                W("sync", "spill1", spill_tot[1])
            if k - WSLOTS in blk_done:
                W("sync", *blk_done[k - WSLOTS])
            W("sync", "cast_" + nm, 16)
            src = wcache[nm][goff * 128:(goff + ncols) * 128].rearrange(
                "(p cn) -> p cn", p=128)
            dst = wblk[:, slot * 8192: slot * 8192 + ncols]
            hW = S.inc(f"wdma{slot}", 16)
            P.em("sync", lambda e, d=dst, s_=src, h=hW: e.dma_start(
                d, s_).then_inc(h, 16))
            blk_idx[0] += 1
            return k, slot, S.cur(f"wdma{slot}")


        def x2load(j):
            if j >= ffn_tiles:
                return
            if j < 2:
                W("gpsimd", "phase", phase_tgt)
                W("gpsimd", "spill0", spill_tot[0])
                W("gpsimd", "spill1", spill_tot[1])
            else:
                t = j // 2
                W("gpsimd", f"spill{t % 2}", 16 * (t // 2 + 1))
                W("gpsimd", "outrdy", 2 * (j - 1))
            dst = x2f[:, (j % 2) * 2048:(j % 2 + 1) * 2048].rearrange(
                "p (t d) -> p t d", t=2)
            src = x2sp_h[j * TF:(j + 1) * TF, :].rearrange(
                "(t p) d -> p t d", p=128)
            hX = S.inc(f"x2l{j % 2}", 16)
            P.em("gpsimd", lambda e, d=dst, s_=src, h=hX: e.dma_start(
                d, s_).then_inc(h, 16))

        for eng in ("vector", "scalar", "tensor"):
            W(eng, "phase", phase_tgt)
        x2load(0)
        x2load(1)

        def mk_ctx(r1buf, r2buf):
            return {"r1b": r1buf, "r2b": r2buf,
                    "last_r1": {0: 0, 1: 0}, "last_mid": {0: 0, 1: 0},
                    "last_lo": {0: 0, 1: 0}}

        gctx = mk_ctx(r1s, r2s)
        lctx = gctx

        def split3(ctx, src_ap, par, dhi, dmid, dlo, psum_src):
            r1 = ctx["r1b"][:, par * TF:(par + 1) * TF]
            r2 = ctx["r2b"][:, par * TF:(par + 1) * TF]
            if psum_src:
                hs = S.inc("gsp_s")
                P.em("scalar", lambda e, d=dhi, s_=src_ap, h=hs: e.activation(
                    d, s_, AF.Copy).then_inc(h, 1))
                W("vector", "gsp_s", S.cur("gsp_s"))
            else:
                hv0 = S.inc("gsp_v")
                P.em("vector", lambda e, d=dhi, s_=src_ap, h=hv0:
                     e.tensor_copy(d, s_).then_inc(h, 1))
                W("vector", "gsp_v", S.cur("gsp_v"))
            W("vector", "gsp_s", ctx["last_mid"][par])
            hv1 = S.inc("gsp_v")
            ctx["last_r1"][par] = S.cur("gsp_v")
            P.em("vector", lambda e, d=r1, s_=src_ap, hi_=dhi, h=hv1:
                 e.tensor_tensor(d, s_, hi_, ALU.subtract).then_inc(h, 1))
            W("scalar", "gsp_v", S.cur("gsp_v"))
            hs2 = S.inc("gsp_s")
            ctx["last_mid"][par] = S.cur("gsp_s")
            P.em("scalar", lambda e, d=dmid, s_=r1, h=hs2: e.activation(
                d, s_, AF.Copy).then_inc(h, 1))
            W("vector", "gsp_s", S.cur("gsp_s"))
            W("vector", "gsp_v", ctx["last_r1"][par])
            W("vector", "gsp_s", ctx["last_lo"][par])
            hv2 = S.inc("gsp_v")
            P.em("vector", lambda e, d=r2, s_=r1, m_=dmid, h=hv2:
                 e.tensor_tensor(d, s_, m_, ALU.subtract).then_inc(h, 1))
            W("scalar", "gsp_v", S.cur("gsp_v"))
            hs3 = S.inc("gsp_s")
            ctx["last_lo"][par] = S.cur("gsp_s")
            P.em("scalar", lambda e, d=dlo, s_=r2, h=hs3: e.activation(
                d, s_, AF.Copy).then_inc(h, 1))

        l2sp_k = {}
        ln2sp_after = {}
        x2aps = {}
        gA_lo = {}
        gB_lo = {}
        g3_lo = {}

        def ln2_path(j):
            sj = j % 2
            if j > 0:
                for eng in ("vector", "scalar"):
                    W(eng, "s2", 32 * j)
            W("vector", f"x2l{sj}", 16 * (j // 2 + 1))
            P.em("vector", lambda e: e.drain())
            x2c = [x2f[:, sj * 2048 + c * D: sj * 2048 + (c + 1) * D]
                   for c in range(2)]
            x2aps[j] = x2c
            l2c = [ln2tm[:, c * D:(c + 1) * D] for c in range(2)]
            norm_tm(x2c, l2c)
            ln2_done = NOPINC("vector", "nrm_v")

            W("tensor", "nrm_v", ln2_done)
            if j > 0:
                W("tensor", "m7ev", 8 * (j - 1) + 2)
            for fp in range(DC // 2):
                fa, fb = 2 * fp, 2 * fp + 1
                if fp >= 1:
                    W("tensor", "gsp_v", max(lctx["last_r1"][0],
                                             lctx["last_r1"][1]))
                for fc in (fa, fb):
                    bank = pb[fc % 2]
                    for tc in range(2):
                        src = ln2tm[:, tc * D + fc * 128:
                                    tc * D + (fc + 1) * 128]
                        dst = bank[:, tc * 128:(tc + 1) * 128]
                        if fc == fa or tc == 0:
                            P.em("tensor", lambda e, d=dst, s_=src:
                                 e.transpose(d, s_, ident[:, :]))
                        else:
                            hT = S.inc("lnT")
                            P.em("tensor", lambda e, d=dst, s_=src, h=hT:
                                 e.transpose(d, s_, ident[:, :]).then_inc(h, 1))
                lnT_cur = S.cur("lnT")
                srcA, srcB = pb[0][:, 0:TF], pb[1][:, 0:TF]
                r1a = lctx["r1b"][:, 0:TF]
                r1b_ = lctx["r1b"][:, TF:2 * TF]
                r2a = lctx["r2b"][:, 0:TF]
                r2b_ = lctx["r2b"][:, TF:2 * TF]
                dst3 = {}
                for nmb, buf in (("hi", l2hi), ("mid", l2mid), ("lo", l2lo)):
                    dst3[("A", nmb)] = buf[:, fa * TF:(fa + 1) * TF]
                    dst3[("B", nmb)] = buf[:, fb * TF:(fb + 1) * TF]
                W("scalar", "lnT", lnT_cur)
                hsA = S.inc("gsp_s")
                P.em("scalar", lambda e, d=dst3[("A", "hi")], s_=srcA, h=hsA:
                     e.activation(d, s_, AF.Copy).then_inc(h, 1))
                c_hiA = S.cur("gsp_s")
                hsB = S.inc("gsp_s")
                P.em("scalar", lambda e, d=dst3[("B", "hi")], s_=srcB, h=hsB:
                     e.activation(d, s_, AF.Copy).then_inc(h, 1))
                c_hiB = S.cur("gsp_s")
                W("vector", "gsp_s", c_hiA)
                hv = S.inc("gsp_v")
                lctx["last_r1"][0] = S.cur("gsp_v")
                P.em("vector", lambda e, d=r1a, s_=srcA, hi_=dst3[("A", "hi")],
                     h=hv: e.tensor_tensor(d, s_, hi_, ALU.subtract
                                           ).then_inc(h, 1))
                c_r1A = S.cur("gsp_v")
                W("vector", "gsp_s", c_hiB)
                hv = S.inc("gsp_v")
                lctx["last_r1"][1] = S.cur("gsp_v")
                P.em("vector", lambda e, d=r1b_, s_=srcB, hi_=dst3[("B", "hi")],
                     h=hv: e.tensor_tensor(d, s_, hi_, ALU.subtract
                                           ).then_inc(h, 1))
                c_r1B = S.cur("gsp_v")
                W("scalar", "gsp_v", c_r1A)
                hs = S.inc("gsp_s")
                lctx["last_mid"][0] = S.cur("gsp_s")
                P.em("scalar", lambda e, d=dst3[("A", "mid")], s_=r1a, h=hs:
                     e.activation(d, s_, AF.Copy).then_inc(h, 1))
                c_midA = S.cur("gsp_s")
                W("scalar", "gsp_v", c_r1B)
                hs = S.inc("gsp_s")
                lctx["last_mid"][1] = S.cur("gsp_s")
                P.em("scalar", lambda e, d=dst3[("B", "mid")], s_=r1b_, h=hs:
                     e.activation(d, s_, AF.Copy).then_inc(h, 1))
                c_midB = S.cur("gsp_s")
                W("vector", "gsp_s", c_midA)
                W("vector", "gsp_v", c_r1A)
                hv = S.inc("gsp_v")
                P.em("vector", lambda e, d=r2a, s_=r1a, m_=dst3[("A", "mid")],
                     h=hv: e.tensor_tensor(d, s_, m_, ALU.subtract
                                           ).then_inc(h, 1))
                c_r2A = S.cur("gsp_v")
                W("vector", "gsp_s", c_midB)
                W("vector", "gsp_v", c_r1B)
                hv = S.inc("gsp_v")
                P.em("vector", lambda e, d=r2b_, s_=r1b_, m_=dst3[("B", "mid")],
                     h=hv: e.tensor_tensor(d, s_, m_, ALU.subtract
                                           ).then_inc(h, 1))
                c_r2B = S.cur("gsp_v")
                W("scalar", "gsp_v", c_r2A)
                hs = S.inc("gsp_s")
                lctx["last_lo"][0] = S.cur("gsp_s")
                P.em("scalar", lambda e, d=dst3[("A", "lo")], s_=r2a, h=hs:
                     e.activation(d, s_, AF.Copy).then_inc(h, 1))
                W("scalar", "gsp_v", c_r2B)
                hs = S.inc("gsp_s")
                lctx["last_lo"][1] = S.cur("gsp_s")
                P.em("scalar", lambda e, d=dst3[("B", "lo")], s_=r2b_, h=hs:
                     e.activation(d, s_, AF.Copy).then_inc(h, 1))
                NOPINC("scalar", "ln2sp")

        ln2_path(0)
        ln2sp_after[0] = S.cur("ln2sp")

        for j in range(ffn_tiles):
            sj = j % 2
            x2load(j + 2)

            if j > 0:
                W("tensor", "ftev", 2 * (j - 1) + 2)
                for eng in ("vector", "scalar"):
                    W(eng, "m7u", 8 * j)
            W("tensor", "ln2sp", ln2sp_after[j])
            for g in range(8):
                for mat in ("w1", "w2"):
                    k_blk, slot, wtgt = wload(mat + "s", g * DC * 512, DC * 512)
                    W("tensor", f"wdma{slot}", wtgt)
                    bank_base = 0 if mat == "w1" else 4
                    semn = "s1" if mat == "w1" else "s2"
                    for mloc in range(4):
                        m = g * 4 + mloc
                        if g > 0:
                            gsem = "siluev" if mat == "w1" else "gev"
                            W("tensor", gsem, 32 * j + 4 * g + mloc - 3)
                        bank = pb[bank_base + mloc]
                        for k in range(DC):
                            lhs = wblk[:, slot * 8192 + k * 512 + mloc * 128:
                                       slot * 8192 + k * 512 + (mloc + 1) * 128]
                            for si, sb_ in enumerate((l2hi, l2mid, l2lo)):
                                rhs = sb_[:, k * TF:(k + 1) * TF]
                                first = (k == 0 and si == 0)
                                last = (k == DC - 1 and si == 2)
                                hM = S.inc(semn) if last else None
                                def mmX(e, o=bank, l=lhs, r=rhs, f=first,
                                        la=last, h=hM):
                                    ins = e.matmul(o[:, 0:TF], l, r,
                                                   start=f, stop=la)
                                    if h is not None:
                                        ins.then_inc(h, 1)
                                P.em("tensor", mmX)
                    blk_done[k_blk] = (semn, S.cur(semn))
                for mloc in range(4):
                    m = g * 4 + mloc
                    W("scalar", "s1", 32 * j + 4 * g + mloc + 1)
                    if m >= 4:
                        W("scalar", "gev", 32 * j + m - 3)
                    dst = silub[:, mloc * TF:(mloc + 1) * TF]
                    hSi = S.inc("siluev")
                    P.em("scalar", lambda e, d=dst, b=pb[mloc], m=m, h=hSi:
                         e.activation(d, b[:, 0:TF], AF.Silu,
                                      scale=scsb["w1sc"][:, m:m + 1]
                                      ).then_inc(h, 1))
                for mloc in range(4):
                    m = g * 4 + mloc
                    W("vector", "s2", 32 * j + 4 * g + mloc + 1)
                    W("vector", "siluev", 32 * j + 4 * g + mloc + 1)
                    par = m % 2
                    gdst = gscr[:, par * TF:(par + 1) * TF]
                    hG = S.inc("gev")
                    P.em("vector", lambda e, d=gdst, b=pb[4 + mloc], m=m,
                         sl=silub[:, mloc * TF:(mloc + 1) * TF], h=hG:
                         e.scalar_tensor_tensor(
                             d, b[:, 0:TF], scsb["w2sc"][:, m:m + 1], sl,
                             ALU.mult, ALU.mult).then_inc(h, 1))
                    W("vector", "gev", S.cur("gev"))
                    split3(gctx, gdst, par,
                           gahi[:, m * TF:(m + 1) * TF],
                           gamid[:, m * TF:(m + 1) * TF],
                           galo[:, m * TF:(m + 1) * TF], False)
                    gA_lo[m] = S.cur("gsp_s")

            if j > 0:
                for eng in ("vector", "scalar"):
                    W(eng, "m6u", 32 * j)
            for g in range(8):
                if g == 0:
                    W("tensor", "siluev", 32 * (j + 1))
                if g == 1:
                    W("tensor", "gev", 32 * (j + 1))
                for half in range(2):
                    k_blk, slot, wtgt = wload("h1s", (g * HC + half * 16) * 512,
                                              16 * 512)
                    W("tensor", f"wdma{slot}", wtgt)
                    for mloc in range(4):
                        m = g * 4 + mloc
                        bank = pb[(g % 2) * 4 + mloc]
                        if half == 0 and g >= 2:
                            W("tensor", "m5ev", 32 * j + (g - 2) * 4 + mloc + 1)
                        for k in range(16):
                            kabs = half * 16 + k
                            if g == 0 and mloc == 0:
                                W("tensor", "gsp_s", gA_lo[kabs])
                            lhs = wblk[:, slot * 8192 + k * 512 + mloc * 128:
                                       slot * 8192 + k * 512 + (mloc + 1) * 128]
                            for si, sb_ in enumerate((gahi, gamid, galo)):
                                rhs = sb_[:, kabs * TF:(kabs + 1) * TF]
                                first = (half == 0 and k == 0 and si == 0)
                                last = (half == 1 and k == 15 and si == 2)
                                if last:
                                    hM = S.inc("m5u")
                                elif half == 0 and mloc == 3 and k == 15 and si == 2:
                                    hM = S.inc("wuse")
                                else:
                                    hM = None
                                def mm5(e, o=bank, l=lhs, r=rhs, f=first,
                                        la=last, h=hM):
                                    ins = e.matmul(o[:, 0:TF], l, r,
                                                   start=f, stop=la)
                                    if h is not None:
                                        ins.then_inc(h, 1)
                                P.em("tensor", mm5)
                    blk_done[k_blk] = (("m5u", S.cur("m5u")) if half == 1
                                       else ("wuse", S.cur("wuse")))
                for mloc in range(4):
                    m = g * 4 + mloc
                    W("scalar", "m5u", 32 * j + 4 * g + mloc + 1)
                    par = m % 2
                    W("scalar", "gsp_v", gctx["last_r1"][par])
                    gdst = gscr[:, par * TF:(par + 1) * TF]
                    hGe = S.inc("m5ev")
                    P.em("scalar", lambda e, d=gdst, b=pb[(g % 2) * 4 + mloc],
                         m=m, h=hGe: e.activation(
                             d, b[:, 0:TF], AF.Gelu,
                             scale=scsb["h1sc"][:, m:m + 1]).then_inc(h, 1))
                    W("vector", "m5ev", S.cur("m5ev"))
                    split3(gctx, gdst, par,
                           gbhi[:, m * TF:(m + 1) * TF],
                           gbmid[:, m * TF:(m + 1) * TF],
                           gblo[:, m * TF:(m + 1) * TF], False)
                    gB_lo[m] = S.cur("gsp_s")

            for g in range(8):
                for half in range(2):
                    k_blk, slot, wtgt = wload("h2s", (g * HC + half * 16) * 512,
                                              16 * 512)
                    W("tensor", f"wdma{slot}", wtgt)
                    for mloc in range(4):
                        m = g * 4 + mloc
                        bank = pb[(g % 2) * 4 + mloc]
                        if half == 0:
                            if g >= 2:
                                W("tensor", "m6ev",
                                  32 * j + (g - 2) * 4 + mloc + 1)
                            else:
                                W("tensor", "m5ev",
                                  32 * j + (6 + g) * 4 + mloc + 1)
                        for k in range(16):
                            kabs = half * 16 + k
                            if g == 0 and mloc == 0:
                                W("tensor", "gsp_s", gB_lo[kabs])
                            lhs = wblk[:, slot * 8192 + k * 512 + mloc * 128:
                                       slot * 8192 + k * 512 + (mloc + 1) * 128]
                            for si, sb_ in enumerate((gbhi, gbmid, gblo)):
                                rhs = sb_[:, kabs * TF:(kabs + 1) * TF]
                                first = (half == 0 and k == 0 and si == 0)
                                last = (half == 1 and k == 15 and si == 2)
                                if last:
                                    hM = S.inc("m6u")
                                elif half == 0 and mloc == 3 and k == 15 and si == 2:
                                    hM = S.inc("wuse")
                                else:
                                    hM = None
                                def mm6(e, o=bank, l=lhs, r=rhs, f=first,
                                        la=last, h=hM):
                                    ins = e.matmul(o[:, 0:TF], l, r,
                                                   start=f, stop=la)
                                    if h is not None:
                                        ins.then_inc(h, 1)
                                P.em("tensor", mm6)
                    blk_done[k_blk] = (("m6u", S.cur("m6u")) if half == 1
                                       else ("wuse", S.cur("wuse")))
                for mloc in range(4):
                    m = g * 4 + mloc
                    W("scalar", "m6u", 32 * j + 4 * g + mloc + 1)
                    par = m % 2
                    W("scalar", "gsp_v", gctx["last_r1"][par])
                    gdst = gscr[:, par * TF:(par + 1) * TF]
                    hGe = S.inc("m6ev")
                    P.em("scalar", lambda e, d=gdst, b=pb[(g % 2) * 4 + mloc],
                         m=m, h=hGe: e.activation(
                             d, b[:, 0:TF], AF.Gelu,
                             scale=scsb["h2sc"][:, m:m + 1]).then_inc(h, 1))
                    W("vector", "m6ev", S.cur("m6ev"))
                    split3(gctx, gdst, par,
                           gahi[:, m * TF:(m + 1) * TF],
                           gamid[:, m * TF:(m + 1) * TF],
                           galo[:, m * TF:(m + 1) * TF], False)
                    g3_lo[m] = S.cur("gsp_s")

            for g in range(2):
                for half in range(2):
                    k_blk, slot, wtgt = wload("w3s", (g * HC + half * 16) * 512,
                                              16 * 512)
                    W("tensor", f"wdma{slot}", wtgt)
                    for mloc in range(4):
                        m = g * 4 + mloc
                        bank = pb[(g % 2) * 4 + mloc]
                        if half == 0:
                            W("tensor", "m6ev", 32 * j + (6 + g) * 4 + mloc + 1)
                        for k in range(16):
                            kabs = half * 16 + k
                            if g == 0 and mloc == 0:
                                W("tensor", "gsp_s", g3_lo[kabs])
                            lhs = wblk[:, slot * 8192 + k * 512 + mloc * 128:
                                       slot * 8192 + k * 512 + (mloc + 1) * 128]
                            for si, sb_ in enumerate((gahi, gamid, galo)):
                                rhs = sb_[:, kabs * TF:(kabs + 1) * TF]
                                first = (half == 0 and k == 0 and si == 0)
                                last = (half == 1 and k == 15 and si == 2)
                                if last:
                                    hM = S.inc("m7u")
                                elif half == 0 and mloc == 3 and k == 15 and si == 2:
                                    hM = S.inc("wuse")
                                else:
                                    hM = None
                                def mm7(e, o=bank, l=lhs, r=rhs, f=first,
                                        la=last, h=hM):
                                    ins = e.matmul(o[:, 0:TF], l, r,
                                                   start=f, stop=la)
                                    if h is not None:
                                        ins.then_inc(h, 1)
                                P.em("tensor", mm7)
                    blk_done[k_blk] = (("m7u", S.cur("m7u")) if half == 1
                                       else ("wuse", S.cur("wuse")))
                for mloc in range(4):
                    m = g * 4 + mloc
                    W("vector", "m7u", 8 * j + 4 * g + mloc + 1)
                    dst = f_fm[:, m * TF:(m + 1) * TF]
                    hFe = S.inc("m7ev")
                    P.em("vector", lambda e, d=dst, b=pb[(g % 2) * 4 + mloc],
                         m=m, h=hFe: e.tensor_scalar(
                             d, b[:, 0:TF], scsb["w3sc"][:, m:m + 1], None,
                             ALU.mult).then_inc(h, 1))

            if j + 1 < ffn_tiles:
                ln2_path(j + 1)
                ln2sp_after[j + 1] = S.cur("ln2sp")

            W("tensor", "m7ev", 8 * (j + 1))
            for tc in range(2):
                W("tensor", "ftev", 2 * j + tc)
                for fc in range(DC):
                    bank = pb[2] if fc < 4 else pb[3]
                    src = f_fm[:, fc * TF + tc * 128: fc * TF + (tc + 1) * 128]
                    dst = bank[:, (fc % 4) * 128:(fc % 4 + 1) * 128]
                    if fc < DC - 1:
                        P.em("tensor", lambda e, d=dst, s_=src: e.transpose(
                            d, s_, ident[:, :]))
                    else:
                        hT = S.inc("fT")
                        P.em("tensor", lambda e, d=dst, s_=src, h=hT:
                             e.transpose(d, s_, ident[:, :]).then_inc(h, 1))
                W("vector", "fT", S.cur("fT"))
                if tc == 0:
                    P.em("vector", lambda e: e.drain())
                    if j > 0:
                        W("vector", "ost", 16 * j)
                d0 = f_tm[:, tc * D: tc * D + 512]
                d1 = f_tm[:, tc * D + 512: tc * D + 1024]
                P.em("vector", lambda e, d=d0: e.tensor_copy(d, pb[2][:, 0:512]))
                hE = S.inc("ftev")
                P.em("vector", lambda e, d=d1, h=hE: e.tensor_copy(
                    d, pb[3][:, 0:512]).then_inc(h, 1))

            fc_ = [f_tm[:, c * D:(c + 1) * D] for c in range(2)]
            norm_tm(fc_, fc_)
            P.em("vector", lambda e: e.drain())
            for tc in range(2):
                hO = S.inc("outrdy")
                P.em("vector", lambda e, a=fc_[tc], b_=x2aps[j][tc],
                     h=hO: e.tensor_tensor(a, a, b_, ALU.add).then_inc(h, 1))

            W("gpsimd", "outrdy", S.cur("outrdy"))
            dsto = out_h[j * TF:(j + 1) * TF, :].rearrange(
                "(t p) d -> p t d", p=128)
            srco = f_tm[:, :].rearrange("p (t d) -> p t d", t=2)
            hOs = S.inc("ost", 16)
            P.em("gpsimd", lambda e, d=dsto, s_=srco, h=hOs: e.dma_start(
                d, s_).then_inc(h, 16))

        P.em("gpsimd", lambda e, h=S.h["ost"], t=16 * ffn_tiles:
             e.wait_ge(h, t))

        P.emit(nc)

    return nc


def host_arrays(inputs):
    g = {}
    g["bT"] = np.ascontiguousarray(np.asarray(inputs["L0_attn_B"]).T)
    g["wT"] = np.ascontiguousarray(np.asarray(inputs["L0_attn_W"]).T)
    g["Ach"] = np.ascontiguousarray(
        np.asarray(inputs["L0_attn_A"]).reshape(DC, 128).T)
    g["bch"] = np.ascontiguousarray(
        np.asarray(inputs["L0_attn_b"]).reshape(DC, 128).T)

    def strips(w, kc, ngrp, bc):
        wt = np.asarray(w).T
        a = wt.reshape(kc, 128, ngrp, 512)
        blocks = []
        for gi in range(ngrp):
            for h0 in range(0, kc, bc):
                blk = a[h0:h0 + bc, :, gi, :]
                blocks.append(np.ascontiguousarray(
                    blk.transpose(1, 0, 2)).ravel())
        return np.concatenate(blocks)

    g["w1s"] = strips(inputs["L0_ffn_w1_w"], DC, 8, DC)
    g["w2s"] = strips(inputs["L0_ffn_w2_w"], DC, 8, DC)
    g["h1s"] = strips(inputs["L0_ffn_h1_w"], HC, 8, 16)
    g["h2s"] = strips(inputs["L0_ffn_h2_w"], HC, 8, 16)
    g["w3s"] = strips(inputs["L0_ffn_w3_w"], HC, 2, 16)
    g["w1sc"] = np.ascontiguousarray(
        np.asarray(inputs["L0_ffn_w1_s"]).reshape(HC, 128).T)
    g["w2sc"] = np.ascontiguousarray(
        np.asarray(inputs["L0_ffn_w2_s"]).reshape(HC, 128).T)
    g["h1sc"] = np.ascontiguousarray(
        np.asarray(inputs["L0_ffn_h1_s"]).reshape(HC, 128).T)
    g["h2sc"] = np.ascontiguousarray(
        np.asarray(inputs["L0_ffn_h2_s"]).reshape(HC, 128).T)
    g["w3sc"] = np.ascontiguousarray(
        np.asarray(inputs["L0_ffn_w3_s"]).reshape(DC, 128).T)
    return {k: np.asarray(v, dtype=np.float32) for k, v in g.items()}


def kernel(**inputs):
    x = np.asarray(inputs["L0_attn_x"], dtype=np.float32)
    B_, L, _ = x.shape
    n_cores = 8
    chunks_per_batch = n_cores // B_
    chunk = L // chunks_per_batch

    g = host_arrays(inputs)
    nc = build_program(pre_tiles=8, own_tiles=8)

    in_maps = []
    for c in range(n_cores):
        b = c // chunks_per_batch
        q = c % chunks_per_batch
        own = x[b, q * chunk:(q + 1) * chunk]
        if q == 0:
            pre = np.zeros((chunk, D), np.float32)
        else:
            pre = x[b, (q - 1) * chunk: q * chunk]
        xc = np.ascontiguousarray(np.concatenate([pre, own], axis=0))
        m = {"x": xc}
        m.update(g)
        in_maps.append(m)

    res = run_bass_kernel_spmd(nc, in_maps, core_ids=list(range(n_cores)))
    out = np.empty((B_, L, D), dtype=np.float32)
    for c in range(n_cores):
        b = c // chunks_per_batch
        q = c % chunks_per_batch
        out[b, q * chunk:(q + 1) * chunk] = res.results[c]["out"]
    return out


# version 14
# speedup vs baseline: 1.0433x; 1.0433x over previous
import contextlib
import numpy as np

import concourse.bass as bass
import concourse.mybir as mybir
from concourse.bass_utils import run_bass_kernel_spmd

F32 = mybir.dt.float32
BF16 = mybir.dt.bfloat16
AF = mybir.ActivationFunctionType
ALU = mybir.AluOpType

D = 1024
HID = 4096
DC = D // 128
HC = HID // 128
TS = 512
TF = 256
EPS = 1e-5
INVD = 1.0 / D

ENGS = ("sync", "gpsimd", "vector", "scalar", "tensor")


class Plan:
    def __init__(self):
        self.ops = {e: [] for e in ENGS}

    def em(self, eng, fn):
        self.ops[eng].append(fn)

    def emit(self, nc):
        with nc.Block() as block:
            @block.sync
            def _(eng):
                for f in self.ops["sync"]:
                    f(eng)

            @block.gpsimd
            def _(eng):
                for f in self.ops["gpsimd"]:
                    f(eng)

            @block.vector
            def _(eng):
                for f in self.ops["vector"]:
                    f(eng)

            @block.scalar
            def _(eng):
                for f in self.ops["scalar"]:
                    f(eng)

            @block.tensor
            def _(eng):
                for f in self.ops["tensor"]:
                    f(eng)


class Sems:
    def __init__(self, nc, stack, names):
        self.h = {n: stack.enter_context(nc.semaphore(n)) for n in names}
        self.c = {n: 0 for n in names}

    def inc(self, name, n=1):
        self.c[name] += n
        return self.h[name]

    def cur(self, name):
        return self.c[name]


def build_program(pre_tiles=8, own_tiles=8):
    ffn_tiles = own_tiles * (TS // TF)
    n_tiles = pre_tiles + own_tiles
    tok_in = n_tiles * TS
    tok_own = own_tiles * TS

    nc = bass.Bass("TRN2", target_bir_lowering=False, debug=False)

    x_h = nc.dram_tensor("x", [tok_in, D], F32, kind="ExternalInput").ap()
    bT_h = nc.dram_tensor("bT", [D, D], F32, kind="ExternalInput").ap()
    wT_h = nc.dram_tensor("wT", [D, D], F32, kind="ExternalInput").ap()
    ach_h = nc.dram_tensor("Ach", [128, DC], F32, kind="ExternalInput").ap()
    bch_h = nc.dram_tensor("bch", [128, DC], F32, kind="ExternalInput").ap()
    wsrc, wcache = {}, {}
    for nm, sz in (("w1s", DC * HID * 128), ("w2s", DC * HID * 128),
                   ("h1s", HC * HID * 128), ("h2s", HC * HID * 128),
                   ("w3s", HC * D * 128)):
        wsrc[nm] = nc.dram_tensor(nm, [sz], F32, kind="ExternalInput").ap()
        wcache[nm] = nc.dram_tensor(nm + "_bf", [sz], BF16, kind="Internal").ap()
    sc_h = {}
    for nm, cols in (("w1sc", HC), ("w2sc", HC), ("h1sc", HC), ("h2sc", HC),
                     ("w3sc", DC)):
        sc_h[nm] = nc.dram_tensor(nm, [128, cols], F32, kind="ExternalInput").ap()
    x2sp_h = nc.dram_tensor("x2sp", [tok_own, D], F32, kind="Internal").ap()
    out_h = nc.dram_tensor("out", [tok_own, D], F32, kind="ExternalOutput").ap()

    with contextlib.ExitStack() as st:
        S = Sems(nc, st, [
            "ident", "init_dma", "bbf", "lnhi", "x_dma0", "x_dma1", "x_dma2",
            "xfree",
            "cast_w1s", "cast_w2s", "cast_h1s", "cast_h2s", "cast_w3s",
            "nrm_v", "nrm_s",
            "lnT", "lnTev", "m1u", "scan", "m2u", "attnev", "attnT",
            "attnTev", "x2done", "spill0", "spill1", "phase",
            "x2l0", "x2l1", "wdma0", "wdma1", "wdma2", "wuse", "ln2sp",
            "s1", "s2", "siluev", "gev", "gsp_v", "gsp_s",
            "m5u", "m5ev", "m6u", "m6ev", "m7u", "m7ev",
            "fT", "ftev", "outrdy", "ost",
        ])

        sb = lambda name, shape, dt=F32: st.enter_context(
            nc.sbuf_tensor(name, shape, dt))
        ident = sb("ident", [128, 128])
        A_sb = sb("A_sb", [128, DC])
        b_sb = sb("b_sb", [128, DC])
        scsb = {nm: sb(nm + "_sb", [128, sc_h[nm].shape[1]]) for nm in sc_h}
        ms4 = sb("ms4", [128, 8])
        ss4 = sb("ss4", [128, 8])
        istd4 = sb("istd4", [128, 8])
        sqscr = sb("sqscr", [128, D])
        carry0 = sb("carry0", [128, 1])
        pb = [st.enter_context(nc.psum_tensor(f"pb{i}", [128, 512], F32))
              for i in range(8)]

        P = Plan()

        def W(eng, name, tgt, plan=None):
            if tgt >= 1:
                h = S.h[name]
                (plan or P).em(eng, lambda e, h=h, t=tgt: e.wait_ge(h, t))

        def NOPINC(eng, name):
            h = S.inc(name)
            t = S.cur(name)
            P.em(eng, lambda e, h=h: e.drain().then_inc(h, 1))
            return t

        def norm_tm(srcs, dsts):
            n = len(srcs)
            P.em("vector", lambda e: e.drain())
            for c in range(n):
                P.em("vector", lambda e, s_=srcs[c], c=c: e.reduce_sum(
                    ms4[:, c:c + 1], s_, axis=mybir.AxisListType.X))
            for c in range(n):
                scr = dsts[c] if dsts[c] is not srcs[c] else sqscr[:, :]
                if dsts[c] is srcs[c] and c > 0:
                    P.em("vector", lambda e: e.drain())
                P.em("vector", lambda e, s_=srcs[c], d=scr, c=c:
                     e.scalar_tensor_tensor(
                         d, s_, 1.0, s_, ALU.mult, ALU.mult,
                         accum_out=ss4[:, c:c + 1]))
            P.em("vector", lambda e: e.drain())
            P.em("vector", lambda e, n=n: e.tensor_scalar(
                ms4[:, 0:n], ms4[:, 0:n], INVD, None, ALU.mult))
            P.em("vector", lambda e, n=n: e.tensor_scalar(
                ss4[:, 0:n], ss4[:, 0:n], INVD, EPS, ALU.mult, ALU.add))
            P.em("vector", lambda e: e.drain())
            P.em("vector", lambda e, n=n: e.tensor_mul(
                istd4[:, 0:n], ms4[:, 0:n], ms4[:, 0:n]))
            P.em("vector", lambda e: e.drain())
            hv = S.inc("nrm_v")
            P.em("vector", lambda e, h=hv, n=n: e.tensor_sub(
                ss4[:, 0:n], ss4[:, 0:n], istd4[:, 0:n]).then_inc(h, 1))
            W("scalar", "nrm_v", S.cur("nrm_v"))
            hs = S.inc("nrm_s")
            P.em("scalar", lambda e, h=hs, n=n: e.activation(
                ss4[:, 0:n], ss4[:, 0:n], AF.Sqrt).then_inc(h, 1))
            W("vector", "nrm_s", S.cur("nrm_s"))
            P.em("vector", lambda e, n=n: e.reciprocal(
                istd4[:, 0:n], ss4[:, 0:n]))
            P.em("vector", lambda e: e.drain())
            for c in range(n):
                P.em("vector", lambda e, s_=srcs[c], d=dsts[c], c=c:
                     e.tensor_scalar(d, s_, ms4[:, c:c + 1],
                                     istd4[:, c:c + 1], ALU.subtract,
                                     ALU.mult))

        inner = contextlib.ExitStack()
        sbi = lambda name, shape, dt=F32: inner.enter_context(
            nc.sbuf_tensor(name, shape, dt))
        colidx = sbi("colidx", [128, 128])
        rowidx = sbi("rowidx", [128, 1])
        xb = sbi("xb", [128, 3 * 4096])
        tmbuf = sbi("tmbuf", [128, 4096])
        lfm = sbi("lfm", [128, 4096])
        afm = lfm
        h2b = sbi("h2b", [128, 4096])
        hcar = sbi("hcar", [128, DC])
        tmb_ln = sbi("tmb_ln", [128, 4096])
        BT_sb = sbi("BT_sb", [128, DC * D])
        WT_sb = sbi("WT_sb", [128, DC * D])
        B_bf = sbi("B_bf", [128, DC * D], BF16)
        lfm_hi = sbi("lfm_hi", [128, 4096], BF16)
        far_tiles = set(range(max(0, pre_tiles - 2)))

        P.em("gpsimd", lambda e: e.iota(colidx[:, :], [[1, 128]],
                                        channel_multiplier=0,
                                        allow_small_or_imprecise_dtypes=True))
        hI = S.inc("ident")
        P.em("gpsimd", lambda e, h=hI: e.iota(
            rowidx[:, :], [[1, 1]], channel_multiplier=1,
            allow_small_or_imprecise_dtypes=True).then_inc(h, 1))
        ncast_gate = min(n_tiles, 10)
        for sl in range(3):
            cnt = len([i for i in range(ncast_gate) if i % 3 == sl])
            if cnt:
                P.em("gpsimd", lambda e, h=S.h[f"x_dma{sl}"], t=16 * cnt:
                     e.wait_ge(h, t))
        for nm in ("w1s", "w2s", "h1s", "h2s", "w3s"):
            src = wsrc[nm].rearrange("(a b) -> a b", a=16)
            dst = wcache[nm].rearrange("(a b) -> a b", a=16)
            hC = S.inc("cast_" + nm, 16)
            P.em("gpsimd", lambda e, s_=src, d=dst, h=hC: e.dma_start(
                d, s_).then_inc(h, 16))

        W("vector", "ident", 1)
        hI2 = S.inc("ident")
        P.em("vector", lambda e, h=hI2: e.tensor_scalar(
            ident[:, :], colidx[:, :], rowidx[:, 0:1], None,
            ALU.is_equal).then_inc(h, 1))
        P.em("vector", lambda e: e.memset(carry0[:, :], 0.0))
        P.em("vector", lambda e: e.drain())

        def init_dma(dst_ap, src_ap):
            h = S.inc("init_dma", 16)
            P.em("sync", lambda e, d=dst_ap, s_=src_ap, h=h: e.dma_start(
                d, s_).then_inc(h, 16))

        for k in range(DC):
            init_dma(BT_sb[:, k * D:(k + 1) * D], bT_h[k * 128:(k + 1) * 128, :])
        for k in range(DC):
            init_dma(WT_sb[:, k * D:(k + 1) * D], wT_h[k * 128:(k + 1) * 128, :])
        init_dma(A_sb[:, :], ach_h[:, :])
        init_dma(b_sb[:, :], bch_h[:, :])
        for nm in sc_h:
            init_dma(scsb[nm][:, :], sc_h[nm][:, :])
        init_tgt = S.cur("init_dma")
        for eng in ("vector", "scalar", "tensor"):
            W(eng, "init_dma", init_tgt)
        hBf = S.inc("bbf")
        P.em("vector", lambda e, h=hBf: e.tensor_copy(
            B_bf[:, :], BT_sb[:, :]).then_inc(h, 1))

        def xload(i):
            if i >= n_tiles:
                return
            dst = xb[:, (i % 3) * 4096:(i % 3 + 1) * 4096].rearrange(
                "p (t d) -> p t d", t=4)
            src = x_h[i * TS:(i + 1) * TS, :].rearrange("(t p) d -> p t d", p=128)
            W("sync", "xfree", i - 2)
            if i - 3 >= pre_tiles:
                t = i - 3 - pre_tiles
                W("sync", f"spill{t % 2}", 16 * (t // 2 + 1))
            h = S.inc(f"x_dma{i % 3}", 16)
            P.em("sync", lambda e, d=dst, s_=src, h=h: e.dma_start(
                d, s_).then_inc(h, 16))

        xload(0)
        xload(1)

        ln1_nv = {}
        lnb = [tmb_ln[:, c * D:(c + 1) * D] for c in range(4)]

        def ln1_for(i, extra_srcs=None, extra_dsts=None):
            sx2 = i % 3
            W("vector", f"x_dma{sx2}", 16 * (i // 3 + 1))
            W("vector", "lnT", 8 * i)
            P.em("vector", lambda e: e.drain())
            xsrc = [xb[:, sx2 * 4096 + c * D: sx2 * 4096 + (c + 1) * D]
                    for c in range(4)]
            norm_tm((extra_srcs or []) + xsrc, (extra_dsts or []) + lnb)
            ln1_nv[i] = NOPINC("vector", "nrm_v")
            if i < pre_tiles:
                NOPINC("vector", "xfree")

        ln1_for(0)

        for i in range(n_tiles):
            s = i % 2
            is_own = i >= pre_tiles
            oi = i - pre_tiles
            if not is_own:
                xload(i + 2)
            sx = i % 3

            W("tensor", "nrm_v", ln1_nv[i])
            if i == 0:
                W("tensor", "ident", 2)
            for fc in range(DC):
                bank = pb[4 + fc % 2]
                W("tensor", "lnTev", 8 * i + fc - 1)
                for tc in range(4):
                    src = tmb_ln[:, tc * D + fc * 128: tc * D + (fc + 1) * 128]
                    dst = bank[:, tc * 128:(tc + 1) * 128]
                    if tc < 3:
                        P.em("tensor", lambda e, d=dst, s_=src: e.transpose(
                            d, s_, ident[:, :]))
                    else:
                        hT = S.inc("lnT")
                        P.em("tensor", lambda e, d=dst, s_=src, h=hT:
                             e.transpose(d, s_, ident[:, :]).then_inc(h, 1))
                W("vector", "lnT", S.cur("lnT"))
                if i > 0 and fc == 0:
                    W("vector", "m1u", 8 * i)
                    if i - 1 >= pre_tiles:
                        W("vector", "attnT", 4 * (i - 1 - pre_tiles + 1))
                far = i in far_tiles
                dstf = (lfm_hi if far else lfm)[:, fc * 512:(fc + 1) * 512]
                hE = S.inc("lnTev")
                P.em("vector", lambda e, d=dstf, b=bank, h=hE: e.tensor_copy(
                    d, b[:, 0:512]).then_inc(h, 1))

            W("tensor", "lnTev", S.cur("lnTev"))
            if i in far_tiles and (i == 0 or i - 1 not in far_tiles):
                W("tensor", "bbf", 1)
            for m in range(DC):
                bank = pb[m % 2]
                W("tensor", "scan", 8 * i + m - 1)
                for k in range(DC):
                    if i in far_tiles:
                        lhs = B_bf[:, k * D + m * 128: k * D + (m + 1) * 128]
                        rhs = lfm_hi[:, k * 512:(k + 1) * 512]
                    else:
                        lhs = BT_sb[:, k * D + m * 128: k * D + (m + 1) * 128]
                        rhs = lfm[:, k * 512:(k + 1) * 512]
                    hM = S.inc("m1u") if k == DC - 1 else None
                    def mm1(e, o=bank, l=lhs, r=rhs, f=(k == 0),
                            la=(k == DC - 1), h=hM):
                        ins = e.matmul(o[:, 0:512], l, r, start=f, stop=la)
                        if h is not None:
                            ins.then_inc(h, 1)
                    P.em("tensor", mm1)
                W("vector", "m1u", S.cur("m1u"))
                if m == 0:
                    P.em("vector", lambda e: e.drain())
                    if i >= 1 and (i - 1) >= pre_tiles:
                        W("vector", "m2u", 8 * (i - 1 - pre_tiles + 1))
                dsth = h2b[:, m * 512:(m + 1) * 512]
                car = carry0[:, 0:1] if i == 0 else hcar[:, m:m + 1]
                abc = A_sb[:, m:m + 1].broadcast_to([128, 512])
                hS = S.inc("scan")
                P.em("vector", lambda e, d=dsth, a=abc, u=bank, c=car, h=hS:
                     e.tensor_tensor_scan(d, a, u[:, 0:512], c, ALU.mult,
                                          ALU.add).then_inc(h, 1))
                W("vector", "scan", S.cur("scan"))
                P.em("vector", lambda e, d=hcar[:, m:m + 1],
                     s_=h2b[:, m * 512 + 511: m * 512 + 512]:
                     e.tensor_copy(d, s_))

            if not is_own:
                if i + 1 < n_tiles:
                    ln1_for(i + 1)
                continue

            W("tensor", "scan", S.cur("scan"))
            for m in range(DC):
                bank = pb[2 + m % 2]
                W("tensor", "attnev", 8 * oi + m - 1)
                for k in range(DC):
                    lhs = WT_sb[:, k * D + m * 128: k * D + (m + 1) * 128]
                    rhs = h2b[:, k * 512:(k + 1) * 512]
                    hM = S.inc("m2u") if k == DC - 1 else None
                    def mm2(e, o=bank, l=lhs, r=rhs, f=(k == 0),
                            la=(k == DC - 1), h=hM):
                        ins = e.matmul(o[:, 0:512], l, r, start=f, stop=la)
                        if h is not None:
                            ins.then_inc(h, 1)
                    P.em("tensor", mm2)
                W("scalar", "m2u", S.cur("m2u"))
                if oi > 0 and m == 0:
                    W("scalar", "attnT", 4 * oi)
                dsta = afm[:, m * 512:(m + 1) * 512]
                hA = S.inc("attnev")
                P.em("scalar", lambda e, d=dsta, b=bank, m=m, h=hA:
                     e.activation(d, b[:, 0:512], AF.Identity,
                                  bias=b_sb[:, m:m + 1]).then_inc(h, 1))

            W("tensor", "attnev", S.cur("attnev"))
            for tc in range(4):
                W("tensor", "attnTev", 4 * oi + tc)
                for fc in range(DC):
                    bank = pb[6] if fc < 4 else pb[7]
                    src = afm[:, fc * 512 + tc * 128: fc * 512 + (tc + 1) * 128]
                    dst = bank[:, (fc % 4) * 128:(fc % 4 + 1) * 128]
                    if fc < DC - 1:
                        P.em("tensor", lambda e, d=dst, s_=src: e.transpose(
                            d, s_, ident[:, :]))
                    else:
                        hT = S.inc("attnT")
                        P.em("tensor", lambda e, d=dst, s_=src, h=hT:
                             e.transpose(d, s_, ident[:, :]).then_inc(h, 1))
                W("vector", "attnT", S.cur("attnT"))
                if tc == 0:
                    P.em("vector", lambda e: e.drain())
                d0 = tmbuf[:, tc * D: tc * D + 512]
                d1 = tmbuf[:, tc * D + 512: tc * D + 1024]
                P.em("vector", lambda e, d=d0: e.tensor_copy(d, pb[6][:, 0:512]))
                hE = S.inc("attnTev")
                P.em("vector", lambda e, d=d1, h=hE: e.tensor_copy(
                    d, pb[7][:, 0:512]).then_inc(h, 1))

            atb = [tmbuf[:, c * D:(c + 1) * D] for c in range(4)]
            if i + 1 < n_tiles:
                ln1_for(i + 1, extra_srcs=atb, extra_dsts=atb)
            else:
                norm_tm(atb, atb)
            P.em("vector", lambda e: e.drain())
            for c in range(4):
                xc = xb[:, sx * 4096 + c * D: sx * 4096 + (c + 1) * D]
                P.em("vector", lambda e, a=xc, b_=atb[c]: e.tensor_tensor(
                    a, a, b_, ALU.add))
            x2t = NOPINC("vector", "x2done")
            NOPINC("vector", "xfree")

            W("sync", "x2done", x2t)
            W("sync", f"spill{oi % 2}", 16 * (oi // 2))
            dst = x2sp_h[oi * TS:(oi + 1) * TS, :].rearrange(
                "(t p) d -> p t d", p=128)
            src = xb[:, sx * 4096:(sx + 1) * 4096].rearrange(
                "p (t d) -> p t d", t=4)
            hSp = S.inc(f"spill{oi % 2}", 16)
            P.em("sync", lambda e, d=dst, s_=src, h=hSp: e.dma_start(
                d, s_).then_inc(h, 16))
            xload(i + 2)

        for eng in ("vector", "scalar", "tensor"):
            NOPINC(eng, "phase")
        phase_tgt = S.cur("phase")
        spill_tot = {0: 16 * ((own_tiles + 1) // 2), 1: 16 * (own_tiles // 2)}

        P.emit(nc)
        inner.close()

        P = Plan()
        WSLOTS = 3
        wblk = sb("wblk", [128, WSLOTS * 8192], BF16)
        x2f = sb("x2f", [128, 2 * 2048])
        ln2tm = sb("ln2tm", [128, 2048])
        f_tm = sb("f_tm", [128, 2048])
        l2hi = sb("l2hi", [128, DC * TF], BF16)
        l2mid = sb("l2mid", [128, DC * TF], BF16)
        l2lo = sb("l2lo", [128, DC * TF], BF16)
        gahi = sb("gahi", [128, HC * TF], BF16)
        gamid = sb("gamid", [128, HC * TF], BF16)
        galo = sb("galo", [128, HC * TF], BF16)
        gbhi = sb("gbhi", [128, HC * TF], BF16)
        gbmid = sb("gbmid", [128, HC * TF], BF16)
        gblo = sb("gblo", [128, HC * TF], BF16)
        gscr = sb("gscr", [128, 2 * TF])
        r1s = sb("r1s", [128, 2 * TF])
        r2s = sb("r2s", [128, 2 * TF])
        f_fm = sb("f_fm", [128, DC * TF])
        silub = f_fm

        blk_idx = [0]
        blk_done = {}

        def wload(nm, goff, ncols):
            k = blk_idx[0]
            slot = k % WSLOTS
            if k == 0:
                W("sync", "phase", phase_tgt)
                W("sync", "spill0", spill_tot[0])
                W("sync", "spill1", spill_tot[1])
            if k - WSLOTS in blk_done:
                W("sync", *blk_done[k - WSLOTS])
            W("sync", "cast_" + nm, 16)
            src = wcache[nm][goff * 128:(goff + ncols) * 128].rearrange(
                "(p cn) -> p cn", p=128)
            dst = wblk[:, slot * 8192: slot * 8192 + ncols]
            hW = S.inc(f"wdma{slot}", 16)
            P.em("sync", lambda e, d=dst, s_=src, h=hW: e.dma_start(
                d, s_).then_inc(h, 16))
            blk_idx[0] += 1
            return k, slot, S.cur(f"wdma{slot}")


        def x2load(j):
            if j >= ffn_tiles:
                return
            if j < 2:
                W("gpsimd", "phase", phase_tgt)
                W("gpsimd", "spill0", spill_tot[0])
                W("gpsimd", "spill1", spill_tot[1])
            else:
                t = j // 2
                W("gpsimd", f"spill{t % 2}", 16 * (t // 2 + 1))
                W("gpsimd", "outrdy", 2 * (j - 1))
            dst = x2f[:, (j % 2) * 2048:(j % 2 + 1) * 2048].rearrange(
                "p (t d) -> p t d", t=2)
            src = x2sp_h[j * TF:(j + 1) * TF, :].rearrange(
                "(t p) d -> p t d", p=128)
            hX = S.inc(f"x2l{j % 2}", 16)
            P.em("gpsimd", lambda e, d=dst, s_=src, h=hX: e.dma_start(
                d, s_).then_inc(h, 16))

        for eng in ("vector", "scalar", "tensor"):
            W(eng, "phase", phase_tgt)
        x2load(0)
        x2load(1)

        def mk_ctx(r1buf, r2buf):
            return {"r1b": r1buf, "r2b": r2buf,
                    "last_r1": {0: 0, 1: 0}, "last_mid": {0: 0, 1: 0},
                    "last_lo": {0: 0, 1: 0}}

        gctx = mk_ctx(r1s, r2s)
        lctx = gctx

        def split3(ctx, src_ap, par, dhi, dmid, dlo, psum_src):
            r1 = ctx["r1b"][:, par * TF:(par + 1) * TF]
            r2 = ctx["r2b"][:, par * TF:(par + 1) * TF]
            if psum_src:
                hs = S.inc("gsp_s")
                P.em("scalar", lambda e, d=dhi, s_=src_ap, h=hs: e.activation(
                    d, s_, AF.Copy).then_inc(h, 1))
                W("vector", "gsp_s", S.cur("gsp_s"))
            else:
                hv0 = S.inc("gsp_v")
                P.em("vector", lambda e, d=dhi, s_=src_ap, h=hv0:
                     e.tensor_copy(d, s_).then_inc(h, 1))
                W("vector", "gsp_v", S.cur("gsp_v"))
            W("vector", "gsp_s", ctx["last_mid"][par])
            hv1 = S.inc("gsp_v")
            ctx["last_r1"][par] = S.cur("gsp_v")
            P.em("vector", lambda e, d=r1, s_=src_ap, hi_=dhi, h=hv1:
                 e.tensor_tensor(d, s_, hi_, ALU.subtract).then_inc(h, 1))
            W("scalar", "gsp_v", S.cur("gsp_v"))
            hs2 = S.inc("gsp_s")
            ctx["last_mid"][par] = S.cur("gsp_s")
            P.em("scalar", lambda e, d=dmid, s_=r1, h=hs2: e.activation(
                d, s_, AF.Copy).then_inc(h, 1))
            W("vector", "gsp_s", S.cur("gsp_s"))
            W("vector", "gsp_v", ctx["last_r1"][par])
            W("vector", "gsp_s", ctx["last_lo"][par])
            hv2 = S.inc("gsp_v")
            P.em("vector", lambda e, d=r2, s_=r1, m_=dmid, h=hv2:
                 e.tensor_tensor(d, s_, m_, ALU.subtract).then_inc(h, 1))
            W("scalar", "gsp_v", S.cur("gsp_v"))
            hs3 = S.inc("gsp_s")
            ctx["last_lo"][par] = S.cur("gsp_s")
            P.em("scalar", lambda e, d=dlo, s_=r2, h=hs3: e.activation(
                d, s_, AF.Copy).then_inc(h, 1))

        l2sp_k = {}
        ln2sp_after = {}
        x2aps = {}
        gA_lo = {}
        gB_lo = {}
        g3_lo = {}

        def ln2_path(j):
            sj = j % 2
            if j > 0:
                for eng in ("vector", "scalar"):
                    W(eng, "s2", 32 * j)
            W("vector", f"x2l{sj}", 16 * (j // 2 + 1))
            P.em("vector", lambda e: e.drain())
            x2c = [x2f[:, sj * 2048 + c * D: sj * 2048 + (c + 1) * D]
                   for c in range(2)]
            x2aps[j] = x2c
            l2c = [ln2tm[:, c * D:(c + 1) * D] for c in range(2)]
            norm_tm(x2c, l2c)
            ln2_done = NOPINC("vector", "nrm_v")

            W("tensor", "nrm_v", ln2_done)
            if j > 0:
                W("tensor", "m7ev", 8 * (j - 1) + 2)
            for fp in range(DC // 2):
                fa, fb = 2 * fp, 2 * fp + 1
                if fp >= 1:
                    W("tensor", "gsp_v", max(lctx["last_r1"][0],
                                             lctx["last_r1"][1]))
                for fc in (fa, fb):
                    bank = pb[fc % 2]
                    for tc in range(2):
                        src = ln2tm[:, tc * D + fc * 128:
                                    tc * D + (fc + 1) * 128]
                        dst = bank[:, tc * 128:(tc + 1) * 128]
                        if fc == fa or tc == 0:
                            P.em("tensor", lambda e, d=dst, s_=src:
                                 e.transpose(d, s_, ident[:, :]))
                        else:
                            hT = S.inc("lnT")
                            P.em("tensor", lambda e, d=dst, s_=src, h=hT:
                                 e.transpose(d, s_, ident[:, :]).then_inc(h, 1))
                lnT_cur = S.cur("lnT")
                srcA, srcB = pb[0][:, 0:TF], pb[1][:, 0:TF]
                r1a = lctx["r1b"][:, 0:TF]
                r1b_ = lctx["r1b"][:, TF:2 * TF]
                r2a = lctx["r2b"][:, 0:TF]
                r2b_ = lctx["r2b"][:, TF:2 * TF]
                dst3 = {}
                for nmb, buf in (("hi", l2hi), ("mid", l2mid), ("lo", l2lo)):
                    dst3[("A", nmb)] = buf[:, fa * TF:(fa + 1) * TF]
                    dst3[("B", nmb)] = buf[:, fb * TF:(fb + 1) * TF]
                W("scalar", "lnT", lnT_cur)
                hsA = S.inc("gsp_s")
                P.em("scalar", lambda e, d=dst3[("A", "hi")], s_=srcA, h=hsA:
                     e.activation(d, s_, AF.Copy).then_inc(h, 1))
                c_hiA = S.cur("gsp_s")
                hsB = S.inc("gsp_s")
                P.em("scalar", lambda e, d=dst3[("B", "hi")], s_=srcB, h=hsB:
                     e.activation(d, s_, AF.Copy).then_inc(h, 1))
                c_hiB = S.cur("gsp_s")
                W("vector", "gsp_s", c_hiA)
                hv = S.inc("gsp_v")
                lctx["last_r1"][0] = S.cur("gsp_v")
                P.em("vector", lambda e, d=r1a, s_=srcA, hi_=dst3[("A", "hi")],
                     h=hv: e.tensor_tensor(d, s_, hi_, ALU.subtract
                                           ).then_inc(h, 1))
                c_r1A = S.cur("gsp_v")
                W("vector", "gsp_s", c_hiB)
                hv = S.inc("gsp_v")
                lctx["last_r1"][1] = S.cur("gsp_v")
                P.em("vector", lambda e, d=r1b_, s_=srcB, hi_=dst3[("B", "hi")],
                     h=hv: e.tensor_tensor(d, s_, hi_, ALU.subtract
                                           ).then_inc(h, 1))
                c_r1B = S.cur("gsp_v")
                W("scalar", "gsp_v", c_r1A)
                hs = S.inc("gsp_s")
                lctx["last_mid"][0] = S.cur("gsp_s")
                P.em("scalar", lambda e, d=dst3[("A", "mid")], s_=r1a, h=hs:
                     e.activation(d, s_, AF.Copy).then_inc(h, 1))
                c_midA = S.cur("gsp_s")
                W("scalar", "gsp_v", c_r1B)
                hs = S.inc("gsp_s")
                lctx["last_mid"][1] = S.cur("gsp_s")
                P.em("scalar", lambda e, d=dst3[("B", "mid")], s_=r1b_, h=hs:
                     e.activation(d, s_, AF.Copy).then_inc(h, 1))
                c_midB = S.cur("gsp_s")
                W("vector", "gsp_s", c_midA)
                W("vector", "gsp_v", c_r1A)
                hv = S.inc("gsp_v")
                P.em("vector", lambda e, d=r2a, s_=r1a, m_=dst3[("A", "mid")],
                     h=hv: e.tensor_tensor(d, s_, m_, ALU.subtract
                                           ).then_inc(h, 1))
                c_r2A = S.cur("gsp_v")
                W("vector", "gsp_s", c_midB)
                W("vector", "gsp_v", c_r1B)
                hv = S.inc("gsp_v")
                P.em("vector", lambda e, d=r2b_, s_=r1b_, m_=dst3[("B", "mid")],
                     h=hv: e.tensor_tensor(d, s_, m_, ALU.subtract
                                           ).then_inc(h, 1))
                c_r2B = S.cur("gsp_v")
                W("scalar", "gsp_v", c_r2A)
                hs = S.inc("gsp_s")
                lctx["last_lo"][0] = S.cur("gsp_s")
                P.em("scalar", lambda e, d=dst3[("A", "lo")], s_=r2a, h=hs:
                     e.activation(d, s_, AF.Copy).then_inc(h, 1))
                W("scalar", "gsp_v", c_r2B)
                hs = S.inc("gsp_s")
                lctx["last_lo"][1] = S.cur("gsp_s")
                P.em("scalar", lambda e, d=dst3[("B", "lo")], s_=r2b_, h=hs:
                     e.activation(d, s_, AF.Copy).then_inc(h, 1))
                NOPINC("scalar", "ln2sp")

        ln2_path(0)
        ln2sp_after[0] = S.cur("ln2sp")

        for j in range(ffn_tiles):
            sj = j % 2
            x2load(j + 2)

            if j > 0:
                W("tensor", "m7ev", 8 * (j - 1) + 4)
                for eng in ("vector", "scalar"):
                    W(eng, "m7u", 8 * j)
            W("tensor", "ln2sp", ln2sp_after[j])
            for g in range(8):
                for mat in ("w1", "w2"):
                    k_blk, slot, wtgt = wload(mat + "s", g * DC * 512, DC * 512)
                    W("tensor", f"wdma{slot}", wtgt)
                    bank_base = 0 if mat == "w1" else 4
                    semn = "s1" if mat == "w1" else "s2"
                    if mat == "w2" and g == 0 and j > 0:
                        W("tensor", "ftev", 2 * j)
                    for mloc in range(4):
                        m = g * 4 + mloc
                        if g > 0:
                            gsem = "siluev" if mat == "w1" else "gev"
                            W("tensor", gsem, 32 * j + 4 * g + mloc - 3)
                        bank = pb[bank_base + mloc]
                        for k in range(DC):
                            lhs = wblk[:, slot * 8192 + k * 512 + mloc * 128:
                                       slot * 8192 + k * 512 + (mloc + 1) * 128]
                            for si, sb_ in enumerate((l2hi, l2mid, l2lo)):
                                rhs = sb_[:, k * TF:(k + 1) * TF]
                                first = (k == 0 and si == 0)
                                last = (k == DC - 1 and si == 2)
                                hM = S.inc(semn) if last else None
                                def mmX(e, o=bank, l=lhs, r=rhs, f=first,
                                        la=last, h=hM):
                                    ins = e.matmul(o[:, 0:TF], l, r,
                                                   start=f, stop=la)
                                    if h is not None:
                                        ins.then_inc(h, 1)
                                P.em("tensor", mmX)
                    blk_done[k_blk] = (semn, S.cur(semn))
                for mloc in range(4):
                    m = g * 4 + mloc
                    W("scalar", "s1", 32 * j + 4 * g + mloc + 1)
                    if m >= 4:
                        W("scalar", "gev", 32 * j + m - 3)
                    dst = silub[:, mloc * TF:(mloc + 1) * TF]
                    hSi = S.inc("siluev")
                    P.em("scalar", lambda e, d=dst, b=pb[mloc], m=m, h=hSi:
                         e.activation(d, b[:, 0:TF], AF.Silu,
                                      scale=scsb["w1sc"][:, m:m + 1]
                                      ).then_inc(h, 1))
                for mloc in range(4):
                    m = g * 4 + mloc
                    W("vector", "s2", 32 * j + 4 * g + mloc + 1)
                    W("vector", "siluev", 32 * j + 4 * g + mloc + 1)
                    par = m % 2
                    gdst = gscr[:, par * TF:(par + 1) * TF]
                    hG = S.inc("gev")
                    P.em("vector", lambda e, d=gdst, b=pb[4 + mloc], m=m,
                         sl=silub[:, mloc * TF:(mloc + 1) * TF], h=hG:
                         e.scalar_tensor_tensor(
                             d, b[:, 0:TF], scsb["w2sc"][:, m:m + 1], sl,
                             ALU.mult, ALU.mult).then_inc(h, 1))
                    W("vector", "gev", S.cur("gev"))
                    split3(gctx, gdst, par,
                           gahi[:, m * TF:(m + 1) * TF],
                           gamid[:, m * TF:(m + 1) * TF],
                           galo[:, m * TF:(m + 1) * TF], False)
                    gA_lo[m] = S.cur("gsp_s")

            if j > 0:
                for eng in ("vector", "scalar"):
                    W(eng, "m6u", 32 * j)
            for g in range(8):
                if g == 0:
                    W("tensor", "siluev", 32 * (j + 1))
                if g == 1:
                    W("tensor", "gev", 32 * (j + 1))
                for half in range(2):
                    k_blk, slot, wtgt = wload("h1s", (g * HC + half * 16) * 512,
                                              16 * 512)
                    W("tensor", f"wdma{slot}", wtgt)
                    for mloc in range(4):
                        m = g * 4 + mloc
                        bank = pb[(g % 2) * 4 + mloc]
                        if half == 0 and g >= 2:
                            W("tensor", "m5ev", 32 * j + (g - 2) * 4 + mloc + 1)
                        for k in range(16):
                            kabs = half * 16 + k
                            if g == 0 and mloc == 0:
                                W("tensor", "gsp_s", gA_lo[kabs])
                            lhs = wblk[:, slot * 8192 + k * 512 + mloc * 128:
                                       slot * 8192 + k * 512 + (mloc + 1) * 128]
                            for si, sb_ in enumerate((gahi, gamid, galo)):
                                rhs = sb_[:, kabs * TF:(kabs + 1) * TF]
                                first = (half == 0 and k == 0 and si == 0)
                                last = (half == 1 and k == 15 and si == 2)
                                if last:
                                    hM = S.inc("m5u")
                                elif half == 0 and mloc == 3 and k == 15 and si == 2:
                                    hM = S.inc("wuse")
                                else:
                                    hM = None
                                def mm5(e, o=bank, l=lhs, r=rhs, f=first,
                                        la=last, h=hM):
                                    ins = e.matmul(o[:, 0:TF], l, r,
                                                   start=f, stop=la)
                                    if h is not None:
                                        ins.then_inc(h, 1)
                                P.em("tensor", mm5)
                    blk_done[k_blk] = (("m5u", S.cur("m5u")) if half == 1
                                       else ("wuse", S.cur("wuse")))
                for mloc in range(4):
                    m = g * 4 + mloc
                    W("scalar", "m5u", 32 * j + 4 * g + mloc + 1)
                    par = m % 2
                    W("scalar", "gsp_v", gctx["last_r1"][par])
                    gdst = gscr[:, par * TF:(par + 1) * TF]
                    hGe = S.inc("m5ev")
                    P.em("scalar", lambda e, d=gdst, b=pb[(g % 2) * 4 + mloc],
                         m=m, h=hGe: e.activation(
                             d, b[:, 0:TF], AF.Gelu,
                             scale=scsb["h1sc"][:, m:m + 1]).then_inc(h, 1))
                    W("vector", "m5ev", S.cur("m5ev"))
                    split3(gctx, gdst, par,
                           gbhi[:, m * TF:(m + 1) * TF],
                           gbmid[:, m * TF:(m + 1) * TF],
                           gblo[:, m * TF:(m + 1) * TF], False)
                    gB_lo[m] = S.cur("gsp_s")

            for g in range(8):
                for half in range(2):
                    k_blk, slot, wtgt = wload("h2s", (g * HC + half * 16) * 512,
                                              16 * 512)
                    W("tensor", f"wdma{slot}", wtgt)
                    for mloc in range(4):
                        m = g * 4 + mloc
                        bank = pb[(g % 2) * 4 + mloc]
                        if half == 0:
                            if g >= 2:
                                W("tensor", "m6ev",
                                  32 * j + (g - 2) * 4 + mloc + 1)
                            else:
                                W("tensor", "m5ev",
                                  32 * j + (6 + g) * 4 + mloc + 1)
                        for k in range(16):
                            kabs = half * 16 + k
                            if g == 0 and mloc == 0:
                                W("tensor", "gsp_s", gB_lo[kabs])
                            lhs = wblk[:, slot * 8192 + k * 512 + mloc * 128:
                                       slot * 8192 + k * 512 + (mloc + 1) * 128]
                            for si, sb_ in enumerate((gbhi, gbmid, gblo)):
                                rhs = sb_[:, kabs * TF:(kabs + 1) * TF]
                                first = (half == 0 and k == 0 and si == 0)
                                last = (half == 1 and k == 15 and si == 2)
                                if last:
                                    hM = S.inc("m6u")
                                elif half == 0 and mloc == 3 and k == 15 and si == 2:
                                    hM = S.inc("wuse")
                                else:
                                    hM = None
                                def mm6(e, o=bank, l=lhs, r=rhs, f=first,
                                        la=last, h=hM):
                                    ins = e.matmul(o[:, 0:TF], l, r,
                                                   start=f, stop=la)
                                    if h is not None:
                                        ins.then_inc(h, 1)
                                P.em("tensor", mm6)
                    blk_done[k_blk] = (("m6u", S.cur("m6u")) if half == 1
                                       else ("wuse", S.cur("wuse")))
                for mloc in range(4):
                    m = g * 4 + mloc
                    W("scalar", "m6u", 32 * j + 4 * g + mloc + 1)
                    par = m % 2
                    W("scalar", "gsp_v", gctx["last_r1"][par])
                    gdst = gscr[:, par * TF:(par + 1) * TF]
                    hGe = S.inc("m6ev")
                    P.em("scalar", lambda e, d=gdst, b=pb[(g % 2) * 4 + mloc],
                         m=m, h=hGe: e.activation(
                             d, b[:, 0:TF], AF.Gelu,
                             scale=scsb["h2sc"][:, m:m + 1]).then_inc(h, 1))
                    W("vector", "m6ev", S.cur("m6ev"))
                    split3(gctx, gdst, par,
                           gahi[:, m * TF:(m + 1) * TF],
                           gamid[:, m * TF:(m + 1) * TF],
                           galo[:, m * TF:(m + 1) * TF], False)
                    g3_lo[m] = S.cur("gsp_s")

            for g in range(2):
                if g == 1 and j + 1 < ffn_tiles:
                    ln2_path(j + 1)
                    ln2sp_after[j + 1] = S.cur("ln2sp")
                for half in range(2):
                    k_blk, slot, wtgt = wload("w3s", (g * HC + half * 16) * 512,
                                              16 * 512)
                    W("tensor", f"wdma{slot}", wtgt)
                    for mloc in range(4):
                        m = g * 4 + mloc
                        bank = pb[(g % 2) * 4 + mloc]
                        if half == 0:
                            W("tensor", "m6ev", 32 * j + (6 + g) * 4 + mloc + 1)
                        for k in range(16):
                            kabs = half * 16 + k
                            if g == 0 and mloc == 0:
                                W("tensor", "gsp_s", g3_lo[kabs])
                            lhs = wblk[:, slot * 8192 + k * 512 + mloc * 128:
                                       slot * 8192 + k * 512 + (mloc + 1) * 128]
                            for si, sb_ in enumerate((gahi, gamid, galo)):
                                rhs = sb_[:, kabs * TF:(kabs + 1) * TF]
                                first = (half == 0 and k == 0 and si == 0)
                                last = (half == 1 and k == 15 and si == 2)
                                if last:
                                    hM = S.inc("m7u")
                                elif half == 0 and mloc == 3 and k == 15 and si == 2:
                                    hM = S.inc("wuse")
                                else:
                                    hM = None
                                def mm7(e, o=bank, l=lhs, r=rhs, f=first,
                                        la=last, h=hM):
                                    ins = e.matmul(o[:, 0:TF], l, r,
                                                   start=f, stop=la)
                                    if h is not None:
                                        ins.then_inc(h, 1)
                                P.em("tensor", mm7)
                    blk_done[k_blk] = (("m7u", S.cur("m7u")) if half == 1
                                       else ("wuse", S.cur("wuse")))
                for mloc in range(4):
                    m = g * 4 + mloc
                    W("vector", "m7u", 8 * j + 4 * g + mloc + 1)
                    dst = f_fm[:, m * TF:(m + 1) * TF]
                    hFe = S.inc("m7ev")
                    P.em("vector", lambda e, d=dst, b=pb[(g % 2) * 4 + mloc],
                         m=m, h=hFe: e.tensor_scalar(
                             d, b[:, 0:TF], scsb["w3sc"][:, m:m + 1], None,
                             ALU.mult).then_inc(h, 1))

            if j + 1 == ffn_tiles:
                pass

            W("tensor", "m7ev", 8 * (j + 1))
            for tc in range(2):
                W("tensor", "ftev", 2 * j + tc)
                for fc in range(DC):
                    bank = pb[4] if fc < 4 else pb[5]
                    src = f_fm[:, fc * TF + tc * 128: fc * TF + (tc + 1) * 128]
                    dst = bank[:, (fc % 4) * 128:(fc % 4 + 1) * 128]
                    if fc < DC - 1:
                        P.em("tensor", lambda e, d=dst, s_=src: e.transpose(
                            d, s_, ident[:, :]))
                    else:
                        hT = S.inc("fT")
                        P.em("tensor", lambda e, d=dst, s_=src, h=hT:
                             e.transpose(d, s_, ident[:, :]).then_inc(h, 1))
                W("vector", "fT", S.cur("fT"))
                if tc == 0:
                    P.em("vector", lambda e: e.drain())
                    if j > 0:
                        W("vector", "ost", 16 * j)
                d0 = f_tm[:, tc * D: tc * D + 512]
                d1 = f_tm[:, tc * D + 512: tc * D + 1024]
                P.em("vector", lambda e, d=d0: e.tensor_copy(d, pb[4][:, 0:512]))
                hE = S.inc("ftev")
                P.em("vector", lambda e, d=d1, h=hE: e.tensor_copy(
                    d, pb[5][:, 0:512]).then_inc(h, 1))

            fc_ = [f_tm[:, c * D:(c + 1) * D] for c in range(2)]
            norm_tm(fc_, fc_)
            P.em("vector", lambda e: e.drain())
            for tc in range(2):
                hO = S.inc("outrdy")
                P.em("vector", lambda e, a=fc_[tc], b_=x2aps[j][tc],
                     h=hO: e.tensor_tensor(a, a, b_, ALU.add).then_inc(h, 1))

            W("gpsimd", "outrdy", S.cur("outrdy"))
            dsto = out_h[j * TF:(j + 1) * TF, :].rearrange(
                "(t p) d -> p t d", p=128)
            srco = f_tm[:, :].rearrange("p (t d) -> p t d", t=2)
            hOs = S.inc("ost", 16)
            P.em("gpsimd", lambda e, d=dsto, s_=srco, h=hOs: e.dma_start(
                d, s_).then_inc(h, 16))

        P.em("gpsimd", lambda e, h=S.h["ost"], t=16 * ffn_tiles:
             e.wait_ge(h, t))

        P.emit(nc)

    return nc


def host_arrays(inputs):
    g = {}
    g["bT"] = np.ascontiguousarray(np.asarray(inputs["L0_attn_B"]).T)
    g["wT"] = np.ascontiguousarray(np.asarray(inputs["L0_attn_W"]).T)
    g["Ach"] = np.ascontiguousarray(
        np.asarray(inputs["L0_attn_A"]).reshape(DC, 128).T)
    g["bch"] = np.ascontiguousarray(
        np.asarray(inputs["L0_attn_b"]).reshape(DC, 128).T)

    def strips(w, kc, ngrp, bc):
        wt = np.asarray(w).T
        a = wt.reshape(kc, 128, ngrp, 512)
        blocks = []
        for gi in range(ngrp):
            for h0 in range(0, kc, bc):
                blk = a[h0:h0 + bc, :, gi, :]
                blocks.append(np.ascontiguousarray(
                    blk.transpose(1, 0, 2)).ravel())
        return np.concatenate(blocks)

    g["w1s"] = strips(inputs["L0_ffn_w1_w"], DC, 8, DC)
    g["w2s"] = strips(inputs["L0_ffn_w2_w"], DC, 8, DC)
    g["h1s"] = strips(inputs["L0_ffn_h1_w"], HC, 8, 16)
    g["h2s"] = strips(inputs["L0_ffn_h2_w"], HC, 8, 16)
    g["w3s"] = strips(inputs["L0_ffn_w3_w"], HC, 2, 16)
    g["w1sc"] = np.ascontiguousarray(
        np.asarray(inputs["L0_ffn_w1_s"]).reshape(HC, 128).T)
    g["w2sc"] = np.ascontiguousarray(
        np.asarray(inputs["L0_ffn_w2_s"]).reshape(HC, 128).T)
    g["h1sc"] = np.ascontiguousarray(
        np.asarray(inputs["L0_ffn_h1_s"]).reshape(HC, 128).T)
    g["h2sc"] = np.ascontiguousarray(
        np.asarray(inputs["L0_ffn_h2_s"]).reshape(HC, 128).T)
    g["w3sc"] = np.ascontiguousarray(
        np.asarray(inputs["L0_ffn_w3_s"]).reshape(DC, 128).T)
    return {k: np.asarray(v, dtype=np.float32) for k, v in g.items()}


def kernel(**inputs):
    x = np.asarray(inputs["L0_attn_x"], dtype=np.float32)
    B_, L, _ = x.shape
    n_cores = 8
    chunks_per_batch = n_cores // B_
    chunk = L // chunks_per_batch

    g = host_arrays(inputs)
    nc = build_program(pre_tiles=8, own_tiles=8)

    in_maps = []
    for c in range(n_cores):
        b = c // chunks_per_batch
        q = c % chunks_per_batch
        own = x[b, q * chunk:(q + 1) * chunk]
        if q == 0:
            pre = np.zeros((chunk, D), np.float32)
        else:
            pre = x[b, (q - 1) * chunk: q * chunk]
        xc = np.ascontiguousarray(np.concatenate([pre, own], axis=0))
        m = {"x": xc}
        m.update(g)
        in_maps.append(m)

    res = run_bass_kernel_spmd(nc, in_maps, core_ids=list(range(n_cores)))
    out = np.empty((B_, L, D), dtype=np.float32)
    for c in range(n_cores):
        b = c // chunks_per_batch
        q = c % chunks_per_batch
        out[b, q * chunk:(q + 1) * chunk] = res.results[c]["out"]
    return out
